# version 13
# baseline (speedup 1.0000x reference)
import numpy as np
import ml_dtypes

import concourse.bass as bass
import concourse.mybir as mybir
from concourse.bass_utils import run_bass_kernel_spmd
from contextlib import ExitStack

F32 = mybir.dt.float32
BF16 = mybir.dt.bfloat16
I32 = mybir.dt.int32
U8 = mybir.dt.uint8
AF = mybir.ActivationFunctionType
ALU = mybir.AluOpType
AX = mybir.AxisListType

P = 128
D = 2048
S = 2048
TOK = 1024
KVD = 512
NH, NKV, HD = 32, 8, 64
II = 8192
EPS = 1e-5
KB = 1024

O_QT = 0
O_KT = 32 * KB
O_V65 = 48 * KB
O_X2 = 0
O_WKB = 66 * KB
O_WVB = 82 * KB
O_CTXT = 66 * KB
O_H2T = 66 * KB
O_HTB = 98 * KB
O_XT = 130 * KB
O_XSQ = 182 * KB
O_WQB = 146 * KB
O_PT = 98 * KB
O_WOB = 98 * KB
O_DT = 98 * KB
O_HMID = 130 * KB
O_GUB = 146 * KB
O_RB = 162 * KB
O_XR = 162 * KB
O_BOB = 182 * KB
O_DSB = 162 * KB
O_GSIL = 170 * KB
O_USC = 172 * KB
O_HB = 174 * KB
O_XSQ2 = 182 * KB
ARENA_BYTES = 190 * KB


def _pair_of_head(h):
    j2, r = divmod(h, 8)
    if r < 4:
        return 4 * j2 + r, 0
    return 4 * j2 + (r - 4), 64


def _heads_of_pair(pj):
    j2, i = divmod(pj, 4)
    return 8 * j2 + i, 8 * j2 + 4 + i


def build_program(debug=None):
    nc = bass.Bass("TRN2", target_bir_lowering=False)

    xt = nc.dram_tensor("xt", [S, D], F32, kind="ExternalInput")
    wq = nc.dram_tensor("wq", [D, D], F32, kind="ExternalInput")
    wk = nc.dram_tensor("wk", [D, KVD], F32, kind="ExternalInput")
    wv = nc.dram_tensor("wv", [D, KVD], F32, kind="ExternalInput")
    wo = nc.dram_tensor("wo", [D, D], F32, kind="ExternalInput")
    bq2 = nc.dram_tensor("bq2", [P, 16], F32, kind="ExternalInput")
    bk2 = nc.dram_tensor("bk2", [P, 4], F32, kind="ExternalInput")
    bvb = nc.dram_tensor("bvb", [P, KVD], F32, kind="ExternalInput")
    bob = nc.dram_tensor("bob", [P, D], F32, kind="ExternalInput")
    gwt = nc.dram_tensor("gwt", [D, II], I32, kind="ExternalInput")
    uwt = nc.dram_tensor("uwt", [D, II], I32, kind="ExternalInput")
    dwt = nc.dram_tensor("dwt", [II, D], I32, kind="ExternalInput")
    gs2 = nc.dram_tensor("gs2", [P, 64], F32, kind="ExternalInput")
    us2 = nc.dram_tensor("us2", [P, 64], F32, kind="ExternalInput")
    dsb_d = nc.dram_tensor("dsb", [P, D], F32, kind="ExternalInput")
    ident_d = nc.dram_tensor("ident", [P, P], BF16, kind="ExternalInput")
    ones_d = nc.dram_tensor("ones", [P, 512], BF16, kind="ExternalInput")
    out_d = nc.dram_tensor("out", [TOK, D], F32, kind="ExternalOutput")
    scr_r = nc.dram_tensor("scr_r", [4, 1024], F32, kind="Internal")

    dbg_outs = {}
    if debug is not None:
        for nm, shp in debug["outs"].items():
            dbg_outs[nm] = nc.dram_tensor("dbg_" + nm, shp, F32,
                                          kind="ExternalOutput")

    with ExitStack() as ctx:
        def sbuf(name, shape, dtype):
            return ctx.enter_context(nc.sbuf_tensor(name, shape, dtype))

        def psum(name, shape, dtype):
            return ctx.enter_context(nc.psum_tensor(name, shape, dtype))

        arena = sbuf("arena", [P, ARENA_BYTES], U8)
        vb = arena.bitcast(BF16)
        vf = arena.bitcast(F32)

        def bfv(off, cols):
            return vb[:, off // 2: off // 2 + cols]

        def f32v(off, cols):
            return vf[:, off // 4: off // 4 + cols]

        identb = sbuf("identb", [P, P], BF16)
        onesb = sbuf("onesb", [P, 512], BF16)
        bq2b = sbuf("bq2b", [P, 16], F32)
        bk2b = sbuf("bk2b", [P, 4], F32)
        bvbb = sbuf("bvbb", [P, KVD], F32)
        gs2b = sbuf("gs2b", [P, 64], F32)
        us2b = sbuf("us2b", [P, 64], F32)
        usb = sbuf("usb", [P, 8], F32)
        st0 = sbuf("st0", [P, 8], F32)
        st1 = sbuf("st1", [P, 8], F32)
        st2 = sbuf("st2", [P, 8], F32)
        st3 = sbuf("st3", [P, 8], F32)
        epsb = sbuf("epsb", [P, 2], F32)
        rrow = sbuf("rrow", [1, 1024], F32)
        rseg = sbuf("rseg", [P, 8], F32)
        rsegi = sbuf("rsegi", [P, 8], F32)
        dscr = sbuf("dscr", [P, 1024], F32)

        PS_A = psum("PS_A", [P, 1024], F32)
        PS_B = psum("PS_B", [P, 1024], F32)
        PS_C = psum("PS_C", [P, 1024], F32)
        PS_D = psum("PS_D", [P, 1024], F32)
        PS_TR = PS_D.bitcast(BF16)
        PS_ROT = [PS_A, PS_B, PS_D]

        SEM = {}
        for _s in ["XDC", "XD0", "XD1", "XDR", "WKV", "WQ0", "WQ1", "WQ2", "WQ3",
                   "WWO", "WWO2", "WWO3", "WG0", "WG1", "WDT", "PE", "ACT", "DVE",
                   "OD"]:
            SEM[_s] = ctx.enter_context(nc.semaphore(_s))
        PE = SEM["PE"]
        ACT = SEM["ACT"]
        DVE = SEM["DVE"]
        OD = SEM["OD"]

        EL = {"t": [], "v": [], "s": [], "y": [], "g": []}
        C = dict(pe=0, act=0, dve=0, od=0)
        DC = {k: 0 for k in SEM}

        def y_dma(sname, out, in_):
            s = SEM[sname]
            EL["y"].append(lambda e, o=out, i=in_:
                           e.dma_start(o, i).then_inc(s, 16))
            DC[sname] += 16
            return DC[sname]

        def o_dma(out, in_):
            EL["y"].append(lambda e, o=out, i=in_:
                           e.dma_start(o, i).then_inc(OD, 16))
            C['od'] += 16
            return C['od']

        def dbg_dma(out, in_):
            EL["g"].append(lambda e, o=out, i=in_:
                           e.dma_start(o, i).then_inc(OD, 16))
            C['od'] += 16
            return C['od']

        def g_dma(sname, out, in_):
            s = SEM[sname]
            EL["g"].append(lambda e, o=out, i=in_:
                           e.dma_start(o, i).then_inc(s, 16))
            DC[sname] += 16
            return DC[sname]

        def _wait(lst, sem, val):
            if val > 0:
                if isinstance(sem, str):
                    sem = SEM[sem]
                lst.append(lambda e, s=sem, v=val: e.wait_ge(s, v))

        def t_wait(sem, val):
            _wait(EL["t"], sem, val)

        def v_wait(sem, val):
            _wait(EL["v"], sem, val)

        def s_wait(sem, val):
            _wait(EL["s"], sem, val)

        def y_wait(sem, val):
            _wait(EL["y"], sem, val)

        def g_wait(sem, val):
            _wait(EL["g"], sem, val)

        def mm(out, l, r, start, stop, inc=False, tp=None):
            if inc:
                C['pe'] += 1
                EL["t"].append(lambda e, o=out, ll=l, rr=r, a=start, b=stop, t=tp:
                               e.matmul(o, ll, rr, start=a, stop=b,
                                        tile_position=t).then_inc(PE, 1))
            else:
                EL["t"].append(lambda e, o=out, ll=l, rr=r, a=start, b=stop, t=tp:
                               e.matmul(o, ll, rr, start=a, stop=b,
                                        tile_position=t))
            return C['pe']

        def t_trans(out, in_, inc=False):
            if inc:
                C['pe'] += 1
                EL["t"].append(lambda e, o=out, i=in_:
                               e.transpose(o, i, identb[:]).then_inc(PE, 1))
            else:
                EL["t"].append(lambda e, o=out, i=in_:
                               e.transpose(o, i, identb[:]))
            return C['pe']

        PE_MODE = ["?"]

        def t_mode(m):
            if PE_MODE[0] != m:
                t_wait(PE, C['pe'])
                PE_MODE[0] = m

        def s_act(out, in_, func, bias=0.0, scale=1.0, inc=False,
                  chain=False):
            if chain:
                s_wait(ACT, C['act'])
            if inc:
                C['act'] += 1
                EL["s"].append(lambda e, o=out, i=in_, f=func, b=bias, s=scale:
                               e.activation(o, i, f, bias=b, scale=s)
                               .then_inc(ACT, 1))
            else:
                EL["s"].append(lambda e, o=out, i=in_, f=func, b=bias, s=scale:
                               e.activation(o, i, f, bias=b, scale=s))
            return C['act']

        def v_op(name, *args, inc=False, chain=False):
            if chain:
                v_wait(DVE, C['dve'])
            if inc:
                C['dve'] += 1
                EL["v"].append(lambda e, n=name, a=args:
                               getattr(e, n)(*a).then_inc(DVE, 1))
            else:
                EL["v"].append(lambda e, n=name, a=args: getattr(e, n)(*a))
            return C['dve']

        def QT(pj):
            return bfv(O_QT + pj * 2 * KB, 1024)

        def KT(j2):
            return bfv(O_KT + j2 * 4 * KB, 2048)

        def V65(t):
            return bfv(O_V65 + t * 1040, 520)

        def X2(tm):
            return f32v(O_X2 + tm * 8 * KB, D)

        def CTXT(dc):
            return bfv(O_CTXT + dc * 2 * KB, 1024)

        def H2T(hc):
            return bfv(O_H2T + hc * 2 * KB, 1024)

        def HTB(par, strip):
            return bfv(O_HTB + par * 16 * KB + strip * KB, 512)

        def HTB_all(par):
            return vb[:, (O_HTB + par * 16 * KB) // 2:
                      (O_HTB + par * 16 * KB) // 2 + 16 * 512]

        def XTile(par):
            return f32v(O_XT + par * 8 * KB, D)

        XSQ = f32v(O_XSQ, D)
        XSQ2 = f32v(O_XSQ2, D)

        def WQB(sl4):
            return bfv(O_WQB + sl4 * 4 * KB, D)

        WKB = bfv(O_WKB, II)
        WVB = bfv(O_WVB, II)

        def PT(par, kc):
            return bfv(O_PT + par * 32 * KB + kc * 2 * KB, 1024)

        def WOB(dc):
            if dc < 3:
                return bfv(170 * KB + dc * 4 * KB, D)
            if dc < 11:
                return bfv(O_WOB + (dc - 3) * 4 * KB, D)
            return bfv(130 * KB + (dc - 11) * 4 * KB, D)

        def WOB_span(dc0, ndc):
            a = WOB(dc0)
            return vb[:, a.offset: a.offset + ndc * D]

        def DT(sl):
            return bfv(O_DT + sl * 4 * KB, D)

        def HMID(sl):
            return bfv(O_HMID + sl * 2 * KB, 1024)

        def GUB(par, which):
            return bfv(O_GUB + par * 8 * KB + which * 4 * KB, D)

        def RB(i):
            off = (O_RB + i * 2 * KB) // 4
            return vf[0:64, off: off + 512]

        O_CTU = 166 * KB

        def CTU(par):
            return bfv(O_CTU + par * 2 * KB, 1024)

        def XRb(par):
            return f32v((150 if par == 0 else 158) * KB, D)
        BOB = f32v(O_BOB, D)
        DSB = f32v(O_DSB, D)
        GSIL = bfv(O_GSIL, 1024)
        USC = bfv(O_USC, 1024)

        def HB(par):
            return bfv(O_HB + par * 4 * KB, D)

        def HBA(sl4):
            return bfv(162 * KB + sl4 * 4 * KB, D)

        y_dma("XDC", identb[:], ident_d[:])
        y_dma("XDC", onesb[:], ones_d[:])
        y_dma("XDC", bq2b[:], bq2[:])
        y_dma("XDC", bk2b[:], bk2[:])
        y_dma("XDC", bvbb[:], bvb[:])
        y_dma("XDC", gs2b[:], gs2[:])
        xd_consts = y_dma("XDC", us2b[:], us2[:])

        g_dma("WKV", WKB.rearrange("p (c n) -> p c n", c=16),
              wk.rearrange("(c p) n -> p c n", p=P))
        wd_wkv = g_dma("WKV", WVB.rearrange("p (c n) -> p c n", c=16),
                       wv.rearrange("(c p) n -> p c n", p=P))

        v_wait("XDC", xd_consts)
        v_op("memset", epsb[:, 0:1], D * EPS)
        v_op("memset", epsb[:, 1:2], 0.5 * float(np.log(D)))
        for t in range(16):
            v_op("memset",
                 V65(t).rearrange("p (k n) -> p k n", k=8)[:, :, 64:65], 1.0)
        v_op("memset", usb[:], 0.0, inc=True)

        wd_wq = {}

        def prefetch_wq(blk, mi, war_pe):
            g_wait(PE, war_pe)
            wd_wq[(blk, mi)] = ("WQ%d" % (mi % 4), g_dma(
                "WQ%d" % (mi % 4),
                WQB(mi % 4).rearrange("p (c n) -> p c n", c=16),
                wq[:, mi * P:(mi + 1) * P].rearrange("(c p) n -> p c n", p=P)))

        for _m in range(4):
            prefetch_wq(0, _m, 0)

        act_xfree, dve_sfree, dve_norm1, act_rstd = {}, {}, {}, {}
        dve_trevac, dve_var, pe_tr = {}, {}, {}
        pe_q, dve_qevac, pe_k, dve_kevac, pe_v, dve_vevac = {}, {}, {}, {}, {}, {}

        def emit_ln(xtile_ap, st, xsq_ap, hb_ap, war_sq_dve, war_st_act,
                    war_st_dve, war_hb_pe):
            v_wait(ACT, war_st_act)
            dve_s = v_op("reduce_sum", st[:, 0:1], xtile_ap, AX.X, inc=True)
            s_wait(DVE, war_sq_dve)
            C['act'] += 1
            EL["s"].append(lambda e, o=xsq_ap, i=xtile_ap, a=st[:, 1:2]:
                           e.activation(o, i, AF.Square, accum_out=a)
                           .then_inc(ACT, 1))
            a_sq = C['act']
            v_op("tensor_mul", st[:, 4:5], st[:, 0:1], st[:, 0:1],
                 inc=True, chain=True)
            v_wait(ACT, a_sq)
            dve_v = v_op("scalar_tensor_tensor", st[:, 5:6], st[:, 4:5],
                         -1.0 / D, st[:, 1:2], ALU.mult, ALU.add,
                         inc=True, chain=True)
            v_op("tensor_scalar_mul", st[:, 2:3], st[:, 0:1], 1.0 / D,
                 inc=True, chain=True)
            s_wait(DVE, dve_v)
            s_wait(DVE, war_st_dve)
            s_act(st[:, 6:7], st[:, 5:6], AF.Ln, bias=epsb[:, 0:1], inc=True)
            a_r = s_act(st[:, 6:7], st[:, 6:7], AF.Exp, scale=-0.5,
                        bias=epsb[:, 1:2], inc=True, chain=True)
            v_wait(ACT, a_r)
            v_wait(PE, war_hb_pe)
            dve_n = v_op("tensor_scalar", hb_ap, xtile_ap, st[:, 2:3],
                         st[:, 6:7], ALU.subtract, ALU.mult, inc=True,
                         chain=True)
            return dve_n, dve_s, dve_v, a_sq, a_r

        pe_qkv_end = {}

        def ln_compute(g):
            blk, bt = divmod(g, 4)
            par = g % 2
            y_wait(ACT, act_xfree.get(g - 2, 0))
            y_wait(DVE, dve_sfree.get(g - 2, 0))
            xsem = "XD%d" % par
            xd_g = y_dma(xsem, XTile(par), xt[g * P:(g + 1) * P, :])
            st = st0 if par == 0 else st1
            v_wait(xsem, xd_g)
            s_wait(xsem, xd_g)
            d_n, dve_s, dve_v, a_sq, a_r = emit_ln(
                XTile(par), st, XSQ, HBA(g % 4),
                war_sq_dve=dve_var.get(g - 2, 0),
                war_st_act=act_rstd.get(g - 2, 0),
                war_st_dve=dve_norm1.get(g - 2, 0),
                war_hb_pe=pe_tr.get((g - 4, 1), 0))
            dve_norm1[g] = d_n
            act_rstd[g] = a_r
            act_xfree[g] = a_sq
            dve_sfree[g] = d_n
            dve_var[g] = dve_v

        def tr_tile(g):
            blk, bt = divmod(g, 4)
            t_mode("tr")
            t_wait(DVE, dve_norm1[g])
            for half in range(2):
                t_wait(DVE, dve_trevac.get((g - 1, half), 0))
                for cc in range(8):
                    c8 = half * 8 + cc
                    t_trans(PS_TR[:, half * 1024 + cc * P:
                                  half * 1024 + (cc + 1) * P],
                            HBA(g % 4)[:, c8 * P:(c8 + 1) * P], inc=(cc == 7))
                pe_tr[(g, half)] = C['pe']
                v_wait(PE, pe_tr[(g, half)])
                v_wait(PE, pe_qkv_end.get(blk - 2, 0))
                src_ = PS_TR[:, half * 1024:(half + 1) * 1024] \
                    .rearrange("p (c n) -> p c n", c=8)
                dst = HTB_all(blk % 2).rearrange("p (c n) -> p c n", c=16) \
                    [:, half * 8:(half + 1) * 8, bt * P:(bt + 1) * P]
                dve_trevac[(g, half)] = v_op("tensor_copy", dst, src_,
                                             inc=True)

        def qkv_block(blk, bg=()):
            bg = list(bg)

            def bg_step():
                if bg:
                    ln_compute(bg.pop(0))
            g = blk * 4 + 3
            t_mode("mm")
            t_wait(DVE, dve_trevac[(g, 1)])
            if blk == 0:
                t_wait("WKV", wd_wkv)
            hpar = blk % 2
            own = blk < 2

            kvps = [PS_B[:, 0:512], PS_B[:, 512:1024],
                    PS_C[:, 0:512], PS_C[:, 512:1024]]
            for miq in range(4):
                ph = kvps[miq]
                t_wait(DVE, dve_vevac.get((blk - 1, miq), 0))
                for ci in range(16):
                    mm(ph, WKB[:, ci * 512 + miq * P:
                               ci * 512 + miq * P + P],
                       HTB(hpar, ci), start=(ci == 0), stop=(ci == 15),
                       inc=(ci == 15))
                pe_k[(blk, miq)] = C['pe']
                v_wait(PE, pe_k[(blk, miq)])
                dve_kevac[(blk, miq)] = v_op(
                    "tensor_scalar_add",
                    KT(miq)[:, blk * 512:(blk + 1) * 512], ph,
                    bk2b[:, miq:miq + 1], inc=True)
            bg_step()

            for tt_ in range(4):
                gt = blk * 4 + tt_
                ph = kvps[tt_]
                t_wait(DVE, dve_kevac[(blk, tt_)])
                for ci in range(16):
                    mm(ph, HTB(hpar, ci)[:, tt_ * P:(tt_ + 1) * P],
                       WVB[:, ci * 512: ci * 512 + 512],
                       start=(ci == 0), stop=(ci == 15), inc=(ci == 15))
                pe_v[(blk, tt_)] = C['pe']
                v_wait(PE, pe_v[(blk, tt_)])
                dve_vevac[(blk, tt_)] = v_op(
                    "tensor_add",
                    V65(gt).rearrange("p (k n) -> p k n", k=8)[:, :, 0:64],
                    ph.rearrange("p (k n) -> p k n", k=8),
                    bvbb.rearrange("p (k n) -> p k n", k=8), inc=True)
            bg_step()

            if own:
                for miq in range(16):
                    t_wait(*wd_wq[(blk, miq)])
                    ph = PS_A[:, (miq % 2) * 512:(miq % 2) * 512 + 512]
                    if miq < 2:
                        t_wait(DVE, dve_qevac.get((blk - 1, 14 + miq), 0))
                    else:
                        t_wait(DVE, dve_qevac.get((blk, miq - 2), 0))
                    for ci in range(16):
                        mm(ph, WQB(miq % 4)[:, ci * P:(ci + 1) * P],
                           HTB(hpar, ci), start=(ci == 0), stop=(ci == 15),
                           inc=(ci == 15))
                    pe_q[(blk, miq)] = C['pe']
                    nb, nm = (blk, miq + 4) if miq + 4 < 16 \
                        else (blk + 1, miq + 4 - 16)
                    if nb < 2:
                        prefetch_wq(nb, nm, pe_q[(blk, miq)])
                    v_wait(PE, pe_q[(blk, miq)])
                    for hh in range(2):
                        h = 2 * miq + hh
                        pj, base = _pair_of_head(h)
                        dve_qevac[(blk, miq)] = v_op(
                            "tensor_scalar_add",
                            QT(pj)[base:base + 64,
                                   blk * 512:(blk + 1) * 512],
                            ph[hh * 64:(hh + 1) * 64, :],
                            bq2b[hh * 64:(hh + 1) * 64, miq:miq + 1],
                            inc=(hh == 1))
                    if miq in (3, 7):
                        bg_step()
            while bg:
                bg_step()
            pe_qkv_end[blk] = C['pe']

        for g in range(4):
            ln_compute(g)
        for g in range(4):
            tr_tile(g)
            ln_compute(g + 4)
        for g in range(4, 8):
            tr_tile(g)
        qkv_block(0, bg=(8, 9, 10, 11))
        for g in range(8, 12):
            tr_tile(g)
        qkv_block(1, bg=(12, 13, 14, 15))
        for g in range(12, 16):
            tr_tile(g)
        qkv_block(2)
        qkv_block(3)

        pe_end_A, act_end_A, dve_end_A = C['pe'], C['act'], C['dve']

        if debug is not None and debug["phase"] == "A":
            g_wait(ACT, act_end_A)
            g_wait(DVE, dve_end_A)
            g_wait(PE, pe_end_A)
            for pj in range(16):
                dbg_dma(dbg_outs["qt"][pj * P:(pj + 1) * P, :], QT(pj))
            for j2 in range(4):
                dbg_dma(dbg_outs["kt"][j2 * P:(j2 + 1) * P, :], KT(j2))
            for t in range(16):
                dbg_dma(dbg_outs["v65"][t * P:(t + 1) * P, :], V65(t))
            EL["g"].append(lambda e, v=C['od']: e.wait_ge(OD, v))
            return _finish(nc, EL)

        t_wait(ACT, act_end_A)
        t_wait(DVE, dve_end_A)
        s_wait(PE, pe_end_A)
        s_wait(DVE, dve_end_A)

        g_wait(PE, pe_end_A)
        g_wait(DVE, dve_end_A)
        wd_wo_early = g_dma(
            "WWO", WOB_span(0, 3).rearrange("p (c n) -> p c n", c=3),
            wo[0:384, :].rearrange("(c p) n -> p c n", p=P))

        act_exp, pe_ctx, pe_sc, pe_ctx4 = {}, {}, {}, {}
        dve_rinv, xd_rb, dve_norm = {}, {}, {}
        dve_ctev, xd_rout = {}, {}
        units = [(qh, pj) for qh in range(2) for pj in range(16)]
        actB0 = C['act']

        def _aexp(u, kc):
            if u < 0:
                return 0
            return actB0 + 16 * u + kc + 1

        def emit_unit(u):
            qh, pj = units[u]
            j2 = pj // 4
            if u >= 1:
                qh1, pj1 = units[u - 1]
                hA1, hB1 = _heads_of_pair(pj1)
                kvA1, kvB1 = 2 * (pj1 // 4), 2 * (pj1 // 4) + 1
            for kc in range(16):
                gidx = 16 * u + kc
                pp = PS_ROT[gidx % 3]
                t_wait(ACT, _aexp((gidx - 3) // 16, (gidx - 3) % 16))
                mm(pp[:, 0:512], KT(j2)[0:64, kc * P:(kc + 1) * P],
                   QT(pj)[0:64, qh * 512:(qh + 1) * 512],
                   start=True, stop=True)
                mm(pp[:, 512:1024], KT(j2)[64:128, kc * P:(kc + 1) * P],
                   QT(pj)[64:128, qh * 512:(qh + 1) * 512],
                   start=True, stop=True, inc=True)
                pe_sc[(u, kc)] = C['pe']
                if u >= 1:
                    t_wait(ACT, act_exp[(u - 1, kc)])
                    if kc == 0:
                        t_wait(DVE, dve_ctev.get(u - 2, 0))
                    mm(PS_C[0:65, 0:512],
                       V65(kc)[:, kvA1 * 65: kvA1 * 65 + 65],
                       PT((u - 1) % 2, kc)[:, 0:512],
                       start=(kc == 0), stop=(kc == 15))
                    mm(PS_C[0:65, 512:1024],
                       V65(kc)[:, kvB1 * 65: kvB1 * 65 + 65],
                       PT((u - 1) % 2, kc)[:, 512:1024],
                       start=(kc == 0), stop=(kc == 15),
                       inc=(kc % 4 == 3))
                    if kc % 4 == 3:
                        pe_ctx4[(u - 1, kc // 4)] = C['pe']
            if u >= 1:
                pe_ctx[u - 1] = C['pe']

        def emit_exp(u):
            for kc in range(16):
                if u >= 2:
                    s_wait(PE, pe_ctx4.get((u - 2, kc // 4), 0))
                s_wait(PE, pe_sc[(u, kc)])
                pp = PS_ROT[(16 * u + kc) % 3]
                act_exp[(u, kc)] = s_act(PT(u % 2, kc), pp[:], AF.Exp,
                                         scale=0.125, inc=True)
                assert act_exp[(u, kc)] == _aexp(u, kc)

        def emit_norm(u):
            qh, pj = units[u]
            hA, hB = _heads_of_pair(pj)
            v_wait(PE, pe_ctx[u])
            v_wait(DVE, dve_norm.get(u - 2, 0))
            v_op("tensor_copy", CTU(u % 2)[0:64, 0:512], PS_C[0:64, 0:512],
                 inc=True)
            v_op("tensor_copy", CTU(u % 2)[0:64, 512:1024],
                 PS_C[0:64, 512:1024], inc=True)
            v_wait("XDR", xd_rout.get(u - 1, 0))
            v_op("tensor_copy", rrow[0:1, 0:512], PS_C[64:65, 0:512], inc=True)
            dve_ctev[u] = v_op("tensor_copy", rrow[0:1, 512:1024],
                               PS_C[64:65, 512:1024], inc=True)
            slot = u % 2
            y_wait(DVE, dve_ctev[u])
            xd_rout[u] = y_dma("XDR", scr_r[slot:slot + 1, :], rrow[0:1, :])
            y_wait("XDR", xd_rout[u])
            y_wait(DVE, dve_rinv.get(u - 1, 0))
            xd1 = y_dma("XDR", rseg[:, :],
                        bass.AP(scr_r, slot * 1024, [[8, 128], [1, 8]]))
            v_wait("XDR", xd1)
            dve_rinv[u] = v_op("reciprocal", rsegi[:, :], rseg[:, :], inc=True)
            y_wait(DVE, dve_rinv[u])
            xd2 = y_dma("XDR", bass.AP(scr_r, (2 + slot) * 1024,
                                       [[8, 128], [1, 8]]), rsegi[:, :])
            y_wait("XDR", xd2)
            y_wait(DVE, dve_norm.get(u - 1, 0))
            y_dma("XDR", RB(0), bass.AP(scr_r, (2 + slot) * 1024,
                                        [[0, 64], [1, 512]]))
            xd_rb[u] = y_dma("XDR", RB(1), bass.AP(scr_r,
                                                   (2 + slot) * 1024 + 512,
                                                   [[0, 64], [1, 512]]))
            v_wait("XDR", xd_rb[u])
            for h, rb, col0 in ((hA, RB(0), 0), (hB, RB(1), 512)):
                dc, base = h // 2, (h % 2) * 64
                dve_norm[u] = v_op(
                    "tensor_mul",
                    CTXT(dc)[base:base + 64, qh * 512:(qh + 1) * 512],
                    CTU(u % 2)[0:64, col0:col0 + 512], rb,
                    inc=(col0 == 512), chain=(col0 == 0))

        t_mode("mm")
        emit_unit(0)
        emit_exp(0)
        for u in range(1, 32):
            emit_unit(u)
            emit_exp(u)
            emit_norm(u - 1)
        qh31, pj31 = units[31]
        hA31, hB31 = _heads_of_pair(pj31)
        kvA31, kvB31 = 2 * (pj31 // 4), 2 * (pj31 // 4) + 1
        for kc in range(16):
            t_wait(ACT, act_exp[(31, kc)])
            if kc == 0:
                t_wait(DVE, dve_ctev.get(30, 0))
            mm(PS_C[0:65, 0:512], V65(kc)[:, kvA31 * 65: kvA31 * 65 + 65],
               PT(31 % 2, kc)[:, 0:512], start=(kc == 0), stop=(kc == 15))
            mm(PS_C[0:65, 512:1024], V65(kc)[:, kvB31 * 65: kvB31 * 65 + 65],
               PT(31 % 2, kc)[:, 512:1024],
               start=(kc == 0), stop=(kc == 15), inc=(kc % 4 == 3))
        pe_ctx[31] = C['pe']
        emit_norm(31)

        g_wait(PE, pe_ctx[30])
        wd_wo_mid = g_dma(
            "WWO2", WOB_span(3, 8).rearrange("p (c n) -> p c n", c=8),
            wo[384:1408, :].rearrange("(c p) n -> p c n", p=P))
        g_wait(PE, pe_ctx[31])
        wd_wo_late = g_dma(
            "WWO3", WOB_span(11, 5).rearrange("p (c n) -> p c n", c=5),
            wo[1408:2048, :].rearrange("(c p) n -> p c n", p=P))

        if debug is not None and debug["phase"] == "B":
            g_wait(DVE, dve_norm[31])
            for dc in range(16):
                dbg_dma(dbg_outs["ctxt"][dc * P:(dc + 1) * P, :], CTXT(dc))
            EL["g"].append(lambda e, v=C['od']: e.wait_ge(OD, v))
            return _finish(nc, EL)

        pass
        y_wait(PE, pe_end_A)
        xd_bob = y_dma("XDC", BOB, bob[:])

        norm_bydc = {}
        for uu in range(32):
            _, pju = units[uu]
            for hh_ in _heads_of_pair(pju):
                dcu = hh_ // 2
                norm_bydc[dcu] = max(norm_bydc.get(dcu, 0), dve_norm[uu])

        xd_xr, pe_wo, dve_x2 = {}, {}, {}
        t_mode("mm")
        t_wait(ACT, act_exp[(31, 15)])
        for tm in range(8):
            if tm == 0:
                y_wait(PE, pe_ctx[31])
            if tm == 1:
                y_wait(DVE, dve_norm[31])
            y_wait(DVE, dve_x2.get(tm - 2, 0))
            xd_xr[tm] = y_dma("XD%d" % (tm % 2), XRb(tm % 2),
                              xt[tm * P:(tm + 1) * P, :])
            t_wait(DVE, dve_x2.get(tm - 2, 0))
            if tm == 0:
                t_wait("WWO", wd_wo_early)
            pset = (PS_A, PS_B) if tm % 2 == 0 else (PS_D, PS_C)
            if tm == 1:
                t_wait(DVE, dve_ctev[31])
            for dc in range(16):
                if tm == 0:
                    t_wait(DVE, norm_bydc[dc])
                    if dc == 3:
                        t_wait("WWO2", wd_wo_mid)
                    if dc == 11:
                        t_wait("WWO3", wd_wo_late)
                for n in range(4):
                    ph = pset[0] if n < 2 else pset[1]
                    mm(ph[:, (n % 2) * 512:(n % 2) * 512 + 512],
                       CTXT(dc)[:, tm * P:(tm + 1) * P],
                       WOB(dc)[:, n * 512:(n + 1) * 512],
                       start=(dc == 0), stop=(dc == 15),
                       inc=(dc == 15 and n == 3))
            pe_wo[tm] = C['pe']
            v_wait(PE, pe_wo[tm])
            v_wait("XD%d" % (tm % 2), xd_xr[tm])
            if tm == 0:
                v_wait("XDC", xd_bob)
            for half in range(2):
                ph = pset[half]
                sl = slice(half * 1024, half * 1024 + 1024)
                v_op("scalar_tensor_tensor", X2(tm)[:, sl], ph[:], 1.0,
                     BOB[:, sl], ALU.mult, ALU.add, inc=True)
                dve_x2[tm] = v_op("tensor_add", X2(tm)[:, sl], X2(tm)[:, sl],
                                  XRb(tm % 2)[:, sl], inc=True, chain=True)

        if debug is not None and debug["phase"] == "C":
            g_wait(DVE, dve_x2[7])
            for tm in range(8):
                dbg_dma(dbg_outs["x2"][tm * P:(tm + 1) * P, :], X2(tm))
            EL["g"].append(lambda e, v=C['od']: e.wait_ge(OD, v))
            return _finish(nc, EL)

        y_wait(DVE, dve_x2[7])
        y_wait(PE, pe_wo[7])
        xd_dsb = y_dma("XDC", DSB, dsb_d[:])

        dve_norm2, act_rstd2, dve_tr2, dve_var2, pe_tr2 = {}, {}, {}, {}, {}
        ST4 = [st0, st1, st2, st3]

        def HBD(sl):
            if sl < 2:
                return bfv(174 * KB + sl * 4 * KB, D)
            return bfv(130 * KB + (sl - 2) * 4 * KB, D)

        def ln2_compute(tm):
            st = ST4[tm % 4]
            v_wait(DVE, dve_x2[7])
            s_wait(DVE, dve_x2[7])
            war_act = act_rstd2.get(tm - 4, 0)
            war_dve = dve_norm2.get(tm - 4, 0)
            if tm < 2:
                war_act = max(war_act, act_rstd[14 + tm])
                war_dve = max(war_dve, dve_norm1[14 + tm])
            d_n, _, dve_v, _, a_r = emit_ln(
                X2(tm), st, XSQ2, HBD(tm % 4),
                war_sq_dve=dve_var2.get(tm - 4, 0),
                war_st_act=war_act,
                war_st_dve=war_dve,
                war_hb_pe=pe_tr2.get(tm - 4, 0))
            dve_norm2[tm] = d_n
            act_rstd2[tm] = a_r
            dve_var2[tm] = dve_v

        def tr2_tile(tm):
            t_mode("tr")
            t_wait(DVE, dve_norm2[tm])
            for half in range(2):
                t_wait(DVE, dve_tr2.get((tm - 1, half), 0))
                for cc in range(8):
                    c8 = half * 8 + cc
                    t_trans(PS_TR[:, half * 1024 + cc * P:
                                  half * 1024 + (cc + 1) * P],
                            HBD(tm % 4)[:, c8 * P:(c8 + 1) * P],
                            inc=(cc == 7))
                pe_tr2[tm] = C['pe']
                v_wait(PE, C['pe'])
                src = PS_TR[:, half * 1024:(half + 1) * 1024] \
                    .rearrange("p (c n) -> p c n", c=8)
                dst = vb[:, O_H2T // 2: O_H2T // 2 + 16 * 1024] \
                    .rearrange("p (c n) -> p c n", c=16) \
                    [:, half * 8:(half + 1) * 8, tm * P:(tm + 1) * P]
                dve_tr2[(tm, half)] = v_op("tensor_copy", dst, src, inc=True)

        for tm in range(3):
            ln2_compute(tm)
        for tm in range(3, 8):
            tr2_tile(tm - 3)
            ln2_compute(tm)
        for tm in range(5, 8):
            tr2_tile(tm)

        if debug is not None and debug["phase"] == "D1":
            g_wait(DVE, dve_tr2[(7, 1)])
            for hc in range(16):
                dbg_dma(dbg_outs["h2t"][hc * P:(hc + 1) * P, :], H2T(hc))
            EL["g"].append(lambda e, v=C['od']: e.wait_ge(OD, v))
            return _finish(nc, EL)

        if debug is not None and debug["phase"] == "D3b":
            v_wait(DVE, dve_tr2[(7, 1)])
            srcs = [PS_A[:, 0:512], PS_A[:, 512:1024], PS_B[:, 0:512],
                    PS_B[:, 512:1024], PS_C[:, 0:512], PS_C[:, 512:1024]]
            for i6, s6 in enumerate(srcs):
                if i6 > 0:
                    EL["v"].append(lambda e, v=C['od']: e.wait_ge(OD, v))
                dve_a = v_op("tensor_copy", dscr[:, 0:512], s6, inc=True)
                g_wait(DVE, dve_a)
                dbg_dma(dbg_outs["ps6"][:, i6 * 512:(i6 + 1) * 512],
                        dscr[:, 0:512])
            EL["g"].append(lambda e, v=C['od']: e.wait_ge(OD, v))
            return _finish(nc, EL)

        t_mode("mm")
        t_wait(DVE, dve_tr2[(7, 1)])
        t_wait(DVE, dve_x2[7])
        v_wait("XDC", xd_dsb)

        pe_g, pe_u, act_silu, act_ucopy, dve_hmid = {}, {}, {}, {}, {}
        pe_dwn, dve_devac, dve_uacc = {}, {}, {}
        dve_ufin, dve_fin = {}, {}
        wd_gu, wd_dt = {}, {}

        def prefetch_gu(ic):
            par = ic % 2
            gsem = "WG%d" % par
            if ic < 2:
                g_wait(PE, pe_wo[7])
                g_wait(DVE, dve_x2[7])
            g_wait(PE, pe_g.get(ic - 2, 0))
            g_dma(gsem, GUB(par, 0).rearrange("p (c n) -> p c n", c=16),
                  gwt[:, ic * P:(ic + 1) * P].rearrange("(c p) n -> p c n",
                                                        p=P))
            g_wait(PE, pe_u.get(ic - 2, 0))
            wd_gu[ic] = (gsem, g_dma(
                gsem, GUB(par, 1).rearrange("p (c n) -> p c n", c=16),
                uwt[:, ic * P:(ic + 1) * P].rearrange("(c p) n -> p c n",
                                                      p=P)))

        def prefetch_dt(grp, sl):
            ic = grp * 8 + sl
            if grp == 0:
                g_wait(PE, pe_wo[7])
            else:
                g_wait(PE, pe_dwn.get((grp - 1, 7, 1), 0))
            wd_dt[ic] = g_dma("WDT", DT(sl), dwt[ic * P:(ic + 1) * P, :])

        prefetch_gu(0)
        prefetch_gu(1)

        for grp in range(8):
            for sl in range(8):
                prefetch_dt(grp, sl)
            for sl in range(8):
                ic = grp * 8 + sl
                par = ic % 2
                t_wait(*wd_gu[ic])
                t_wait(ACT, act_silu.get(ic - 1, 0))
                if sl == 0 and grp > 0:
                    t_wait(DVE, dve_devac[(grp - 1, 7, 1)])
                for hc in range(16):
                    for nn in range(2):
                        mm(PS_A[:, nn * 512: nn * 512 + 512],
                           GUB(par, 0)[:, hc * P:(hc + 1) * P],
                           H2T(hc)[:, nn * 512: nn * 512 + 512],
                           start=(hc == 0), stop=(hc == 15),
                           inc=(hc == 15 and nn == 1))
                pe_g[ic] = C['pe']
                if debug is not None and debug["phase"] == "D3" and ic == 0:
                    v_wait(PE, pe_g[0])
                    dve_a = v_op("tensor_copy", dscr[:], PS_A[:], inc=True)
                    g_wait(DVE, dve_a)
                    dbg_dma(dbg_outs["psg"][:, :], dscr[:])
                    EL["g"].append(lambda e, v=C['od']: e.wait_ge(OD, v))
                    return _finish(nc, EL)
                t_wait(ACT, act_ucopy.get(ic - 1, 0))
                if sl == 0:
                    t_wait(DVE, dve_uacc.get(grp - 1, 0))
                for hc in range(16):
                    for nn in range(2):
                        mm(PS_B[:, nn * 512: nn * 512 + 512],
                           GUB(par, 1)[:, hc * P:(hc + 1) * P],
                           H2T(hc)[:, nn * 512: nn * 512 + 512],
                           start=(hc == 0), stop=(hc == 15),
                           inc=(hc == 15 and nn == 1))
                pe_u[ic] = C['pe']
                if ic + 2 < 64:
                    prefetch_gu(ic + 2)
                s_wait(PE, pe_g[ic])
                s_wait(DVE, dve_hmid.get(ic - 1, 0))
                act_silu[ic] = s_act(GSIL, PS_A[:], AF.Silu,
                                     scale=gs2b[:, ic:ic + 1], inc=True)
                s_wait(PE, pe_u[ic])
                act_ucopy[ic] = s_act(USC, PS_B[:], AF.Copy,
                                      scale=us2b[:, ic:ic + 1], inc=True)
                v_wait(ACT, act_ucopy[ic])
                if sl == 0 and grp > 0:
                    v_wait(PE, pe_dwn[(grp - 1, 7, 1)])
                dve_hmid[ic] = v_op("tensor_mul", HMID(sl), GSIL, USC,
                                    inc=True)
                if debug is not None and debug["phase"] == "D2" and ic == 0:
                    v_wait(PE, pe_g[0])
                    dve_a = v_op("tensor_copy", dscr[:], PS_A[:], inc=True)
                    g_wait(DVE, dve_a)
                    dbg_dma(dbg_outs["psg"][:, :], dscr[:])
                    v_wait(PE, pe_u[0])
                    g_wait(DVE, dve_a)
                    dve_b = v_op("tensor_copy", dscr[:], PS_B[:], inc=True)
                    g_wait(DVE, dve_b)
                    dbg_dma(dbg_outs["psu"][:, :], dscr[:])
                    g_wait(DVE, dve_hmid[0])
                    dbg_dma(dbg_outs["gsil"][:, :], GSIL)
                    dbg_dma(dbg_outs["usc"][:, :], USC)
                    dbg_dma(dbg_outs["hmid0"][:, :], HMID(0))
                    dbg_dma(dbg_outs["gub0"][:, :], GUB(0, 0))
                    dbg_dma(dbg_outs["gub1"][:, :], GUB(0, 1))
                    EL["g"].append(lambda e, v=C['od']: e.wait_ge(OD, v))
                    return _finish(nc, EL)

            t_wait(DVE, dve_hmid[grp * 8 + 7])
            t_wait(ACT, act_ucopy[grp * 8 + 7])
            t_wait("WDT", wd_dt[grp * 8 + 7])
            for tm in range(8):
                for dh in range(2):
                    pd = PS_C if dh == 0 else PS_D
                    t_wait(DVE, dve_devac.get(
                        (grp, tm - 1, dh),
                        dve_devac.get((grp - 1, 7, dh), 0)))
                    for sl in range(8):
                        st_ap = HMID(sl)[:, tm * P:(tm + 1) * P]
                        for nn in range(2):
                            mm(pd[:, nn * 512: nn * 512 + 512], st_ap,
                               DT(sl)[:, dh * 1024 + nn * 512:
                                      dh * 1024 + (nn + 1) * 512],
                               start=(sl == 0), stop=(sl == 7),
                               inc=(dh == 1 and sl == 7 and nn == 1))
                        if dh == 0:
                            if grp == 7 and sl == 0:
                                t_wait(DVE, dve_ufin.get(tm - 1, 0))
                            mm(PS_B[:, tm:tm + 1], st_ap, onesb[0:P, 0:1],
                               start=(sl == 0), stop=(sl == 7),
                               inc=(sl == 7))
                    pe_dwn[(grp, tm, dh)] = C['pe']
                    if grp == 7 and dh == 0:
                        v_wait(PE, pe_dwn[(7, tm, 0)])
                        v_op("tensor_add", usb[:, tm:tm + 1],
                             usb[:, tm:tm + 1], PS_B[:, tm:tm + 1],
                             inc=True, chain=True)
                        dve_ufin[tm] = v_op(
                            "tensor_scalar_mul", usb[:, tm:tm + 1],
                            usb[:, tm:tm + 1], -128.0, inc=True, chain=True)
                    v_wait(PE, pe_dwn[(grp, tm, dh)])
                    sl_ = slice(dh * 1024, dh * 1024 + 1024)
                    v_op("tensor_mul", dscr[:], pd[:], DSB[:, sl_],
                         inc=True, chain=True)
                    dve_devac[(grp, tm, dh)] = v_op(
                        "tensor_add", X2(tm)[:, sl_], X2(tm)[:, sl_],
                        dscr[:], inc=True, chain=True)
                    if grp == 7 and dh == 1:
                        dve_fin[tm] = v_op(
                            "scalar_tensor_tensor", X2(tm)[:], DSB[:],
                            usb[:, tm:tm + 1], X2(tm)[:],
                            ALU.mult, ALU.add, inc=True, chain=True)
                        y_wait(DVE, dve_fin[tm])
                        o_dma(out_d[tm * P:(tm + 1) * P, :], X2(tm))
            if grp < 7:
                v_wait(PE, pe_dwn[(grp, 7, 0)])
                dve_uacc[grp] = v_op("tensor_add", usb[:], usb[:],
                                     PS_B[:, 0:8], inc=True, chain=True)

        EL["y"].append(lambda e, v=C['od']: e.wait_ge(OD, v))

        return _finish(nc, EL)


def _finish(nc, EL):
    with nc.Block() as block:
        @block.tensor
        def _(eng):
            for f in EL["t"]:
                f(eng)

        @block.vector
        def _(eng):
            for f in EL["v"]:
                f(eng)

        @block.scalar
        def _(eng):
            for f in EL["s"]:
                f(eng)

        @block.sync
        def _(eng):
            for f in EL["y"]:
                f(eng)

        @block.gpsimd
        def _(eng):
            for f in EL["g"]:
                f(eng)
    return nc


_cache = {}


def _get_program(debug=None):
    key = None if debug is None else debug["phase"]
    if key not in _cache:
        _cache[key] = build_program(debug)
    return _cache[key]


def make_in_maps(inputs):
    x = np.asarray(inputs["L0_attn_x"], np.float32)
    asc = np.ascontiguousarray
    bq = np.asarray(inputs["L0_attn_bq"], np.float32)
    bk = np.asarray(inputs["L0_attn_bk"], np.float32)
    bv = np.asarray(inputs["L0_attn_bv"], np.float32)
    bo = np.asarray(inputs["L0_attn_bo"], np.float32)
    gs = np.asarray(inputs["L0_ffn_gate_scale"], np.float32)
    us = np.asarray(inputs["L0_ffn_up_scale"], np.float32)
    ds = np.asarray(inputs["L0_ffn_down_scale"], np.float32)

    shared = dict(
        wq=asc(np.asarray(inputs["L0_attn_Wq"], np.float32)),
        wk=asc(np.asarray(inputs["L0_attn_Wk"], np.float32)),
        wv=asc(np.asarray(inputs["L0_attn_Wv"], np.float32)),
        wo=asc(np.asarray(inputs["L0_attn_Wo"], np.float32)),
        bq2=asc(bq.reshape(16, P).T),
        bk2=asc(bk.reshape(4, P).T),
        bvb=asc(np.broadcast_to(bv, (P, KVD))),
        bob=asc(np.broadcast_to(bo, (P, D))),
        gwt=asc(np.asarray(inputs["L0_ffn_gate_w"], np.int32).T),
        uwt=asc(np.asarray(inputs["L0_ffn_up_w"], np.int32).T),
        dwt=asc(np.asarray(inputs["L0_ffn_down_w"], np.int32).T),
        gs2=asc(gs.reshape(64, P).T),
        us2=asc(us.reshape(64, P).T),
        dsb=asc(np.broadcast_to(ds, (P, D))),
        ident=np.eye(P, dtype=ml_dtypes.bfloat16),
        ones=np.ones((P, 512), dtype=ml_dtypes.bfloat16),
    )
    in_maps = []
    for c in range(8):
        b, half = divmod(c, 2)
        xc = asc(np.concatenate(
            [x[b, half * TOK:(half + 1) * TOK],
             x[b, (1 - half) * TOK:(2 - half) * TOK]], axis=0))
        m = dict(shared)
        m["xt"] = xc
        in_maps.append(m)
    return in_maps


def kernel(**inputs):
    nc = _get_program()
    in_maps = make_in_maps(inputs)
    res = run_bass_kernel_spmd(nc, in_maps, core_ids=list(range(8)))
    out = np.empty((4, S, D), np.float32)
    for c in range(8):
        b, half = divmod(c, 2)
        out[b, half * TOK:(half + 1) * TOK] = res.results[c]["out"]
    return out


# version 16
# speedup vs baseline: 1.0350x; 1.0350x over previous
import numpy as np
import ml_dtypes

import concourse.bass as bass
import concourse.mybir as mybir
from concourse.bass_utils import run_bass_kernel_spmd
from contextlib import ExitStack

F32 = mybir.dt.float32
BF16 = mybir.dt.bfloat16
I32 = mybir.dt.int32
U8 = mybir.dt.uint8
AF = mybir.ActivationFunctionType
ALU = mybir.AluOpType
AX = mybir.AxisListType

P = 128
D = 2048
S = 2048
TOK = 1024
KVD = 512
NH, NKV, HD = 32, 8, 64
II = 8192
EPS = 1e-5
KB = 1024

O_QT = 0
O_KT = 32 * KB
O_V65 = 48 * KB
O_X2 = 0
O_WKB = 66 * KB
O_WVB = 82 * KB
O_CTXT = 66 * KB
O_H2T = 66 * KB
O_HTB = 98 * KB
O_XT = 130 * KB
O_XSQ = 182 * KB
O_WQB = 146 * KB
O_PT = 98 * KB
O_WOB = 98 * KB
O_DT = 98 * KB
O_HMID = 130 * KB
O_GUB = 146 * KB
O_RB = 162 * KB
O_XR = 162 * KB
O_BOB = 182 * KB
O_DSB = 162 * KB
O_GSIL = 170 * KB
O_USC = 172 * KB
O_HB = 174 * KB
O_XSQ2 = 182 * KB
ARENA_BYTES = 190 * KB


def _pair_of_head(h):
    j2, r = divmod(h, 8)
    if r < 4:
        return 4 * j2 + r, 0
    return 4 * j2 + (r - 4), 64


def _heads_of_pair(pj):
    j2, i = divmod(pj, 4)
    return 8 * j2 + i, 8 * j2 + 4 + i


def build_program(debug=None):
    nc = bass.Bass("TRN2", target_bir_lowering=False)

    xt = nc.dram_tensor("xt", [S, D], F32, kind="ExternalInput")
    wq = nc.dram_tensor("wq", [D, D], F32, kind="ExternalInput")
    wk = nc.dram_tensor("wk", [D, KVD], F32, kind="ExternalInput")
    wv = nc.dram_tensor("wv", [D, KVD], F32, kind="ExternalInput")
    wo = nc.dram_tensor("wo", [D, D], F32, kind="ExternalInput")
    bq2 = nc.dram_tensor("bq2", [P, 16], F32, kind="ExternalInput")
    bk2 = nc.dram_tensor("bk2", [P, 4], F32, kind="ExternalInput")
    bvb = nc.dram_tensor("bvb", [P, KVD], F32, kind="ExternalInput")
    bob = nc.dram_tensor("bob", [P, D], F32, kind="ExternalInput")
    gwt = nc.dram_tensor("gwt", [D, II], I32, kind="ExternalInput")
    uwt = nc.dram_tensor("uwt", [D, II], I32, kind="ExternalInput")
    dwt = nc.dram_tensor("dwt", [II, D], I32, kind="ExternalInput")
    gs2 = nc.dram_tensor("gs2", [P, 64], F32, kind="ExternalInput")
    us2 = nc.dram_tensor("us2", [P, 64], F32, kind="ExternalInput")
    dsb_d = nc.dram_tensor("dsb", [P, D], F32, kind="ExternalInput")
    ident_d = nc.dram_tensor("ident", [P, P], BF16, kind="ExternalInput")
    ones_d = nc.dram_tensor("ones", [P, 512], BF16, kind="ExternalInput")
    out_d = nc.dram_tensor("out", [TOK, D], F32, kind="ExternalOutput")
    scr_r = nc.dram_tensor("scr_r", [4, 1024], F32, kind="Internal")

    dbg_outs = {}
    if debug is not None:
        for nm, shp in debug["outs"].items():
            dbg_outs[nm] = nc.dram_tensor("dbg_" + nm, shp, F32,
                                          kind="ExternalOutput")

    with ExitStack() as ctx:
        def sbuf(name, shape, dtype):
            return ctx.enter_context(nc.sbuf_tensor(name, shape, dtype))

        def psum(name, shape, dtype):
            return ctx.enter_context(nc.psum_tensor(name, shape, dtype))

        arena = sbuf("arena", [P, ARENA_BYTES], U8)
        vb = arena.bitcast(BF16)
        vf = arena.bitcast(F32)

        def bfv(off, cols):
            return vb[:, off // 2: off // 2 + cols]

        def f32v(off, cols):
            return vf[:, off // 4: off // 4 + cols]

        identb = sbuf("identb", [P, P], BF16)
        onesb = sbuf("onesb", [P, 512], BF16)
        bq2b = sbuf("bq2b", [P, 16], F32)
        bk2b = sbuf("bk2b", [P, 4], F32)
        bvbb = sbuf("bvbb", [P, KVD], F32)
        gs2b = sbuf("gs2b", [P, 64], F32)
        us2b = sbuf("us2b", [P, 64], F32)
        usb = sbuf("usb", [P, 8], F32)
        st0 = sbuf("st0", [P, 8], F32)
        st1 = sbuf("st1", [P, 8], F32)
        st2 = sbuf("st2", [P, 8], F32)
        st3 = sbuf("st3", [P, 8], F32)
        epsb = sbuf("epsb", [P, 2], F32)
        rrow = sbuf("rrow", [1, 1024], F32)
        rseg = sbuf("rseg", [P, 8], F32)
        rsegi = sbuf("rsegi", [P, 8], F32)
        dscr = sbuf("dscr", [P, 1024], F32)

        PS_A = psum("PS_A", [P, 1024], F32)
        PS_B = psum("PS_B", [P, 1024], F32)
        PS_C = psum("PS_C", [P, 1024], F32)
        PS_D = psum("PS_D", [P, 1024], F32)
        PS_TR = PS_D.bitcast(BF16)
        PS_ROT = [PS_A, PS_B, PS_D]

        SEM = {}
        for _s in ["XDC", "XDI", "XD0", "XD1", "XDR", "WKV", "WQ0", "WQ1", "WQ2", "WQ3",
                   "WWO", "WWO2", "WWO3", "WG0", "WG1", "WDT", "PE", "ACT", "DVE",
                   "OD"]:
            SEM[_s] = ctx.enter_context(nc.semaphore(_s))
        PE = SEM["PE"]
        ACT = SEM["ACT"]
        DVE = SEM["DVE"]
        OD = SEM["OD"]

        EL = {"t": [], "v": [], "s": [], "y": [], "g": []}
        C = dict(pe=0, act=0, dve=0, od=0)
        DC = {k: 0 for k in SEM}

        def y_dma(sname, out, in_):
            s = SEM[sname]
            EL["y"].append(lambda e, o=out, i=in_:
                           e.dma_start(o, i).then_inc(s, 16))
            DC[sname] += 16
            return DC[sname]

        def o_dma(out, in_):
            EL["y"].append(lambda e, o=out, i=in_:
                           e.dma_start(o, i).then_inc(OD, 16))
            C['od'] += 16
            return C['od']

        def dbg_dma(out, in_):
            EL["g"].append(lambda e, o=out, i=in_:
                           e.dma_start(o, i).then_inc(OD, 16))
            C['od'] += 16
            return C['od']

        def g_dma(sname, out, in_):
            s = SEM[sname]
            EL["g"].append(lambda e, o=out, i=in_:
                           e.dma_start(o, i).then_inc(s, 16))
            DC[sname] += 16
            return DC[sname]

        def _wait(lst, sem, val):
            if val > 0:
                if isinstance(sem, str):
                    sem = SEM[sem]
                lst.append(lambda e, s=sem, v=val: e.wait_ge(s, v))

        def t_wait(sem, val):
            _wait(EL["t"], sem, val)

        def v_wait(sem, val):
            _wait(EL["v"], sem, val)

        def s_wait(sem, val):
            _wait(EL["s"], sem, val)

        def y_wait(sem, val):
            _wait(EL["y"], sem, val)

        def g_wait(sem, val):
            _wait(EL["g"], sem, val)

        def mm(out, l, r, start, stop, inc=False, tp=None):
            if inc:
                C['pe'] += 1
                EL["t"].append(lambda e, o=out, ll=l, rr=r, a=start, b=stop, t=tp:
                               e.matmul(o, ll, rr, start=a, stop=b,
                                        tile_position=t).then_inc(PE, 1))
            else:
                EL["t"].append(lambda e, o=out, ll=l, rr=r, a=start, b=stop, t=tp:
                               e.matmul(o, ll, rr, start=a, stop=b,
                                        tile_position=t))
            return C['pe']

        def t_trans(out, in_, inc=False):
            if inc:
                C['pe'] += 1
                EL["t"].append(lambda e, o=out, i=in_:
                               e.transpose(o, i, identb[:]).then_inc(PE, 1))
            else:
                EL["t"].append(lambda e, o=out, i=in_:
                               e.transpose(o, i, identb[:]))
            return C['pe']

        PE_MODE = ["?"]

        def t_mode(m):
            if PE_MODE[0] != m:
                t_wait(PE, C['pe'])
                PE_MODE[0] = m

        def s_act(out, in_, func, bias=0.0, scale=1.0, inc=False,
                  chain=False):
            if chain:
                s_wait(ACT, C['act'])
            if inc:
                C['act'] += 1
                EL["s"].append(lambda e, o=out, i=in_, f=func, b=bias, s=scale:
                               e.activation(o, i, f, bias=b, scale=s)
                               .then_inc(ACT, 1))
            else:
                EL["s"].append(lambda e, o=out, i=in_, f=func, b=bias, s=scale:
                               e.activation(o, i, f, bias=b, scale=s))
            return C['act']

        def v_op(name, *args, inc=False, chain=False):
            if chain:
                v_wait(DVE, C['dve'])
            if inc:
                C['dve'] += 1
                EL["v"].append(lambda e, n=name, a=args:
                               getattr(e, n)(*a).then_inc(DVE, 1))
            else:
                EL["v"].append(lambda e, n=name, a=args: getattr(e, n)(*a))
            return C['dve']

        def QT(pj):
            return bfv(O_QT + pj * 2 * KB, 1024)

        def KT(j2):
            return bfv(O_KT + j2 * 4 * KB, 2048)

        def V65(t):
            return bfv(O_V65 + t * 1040, 520)

        def X2(tm):
            return f32v(O_X2 + tm * 8 * KB, D)

        def CTXT(dc):
            return bfv(O_CTXT + dc * 2 * KB, 1024)

        def H2T(hc):
            return bfv(O_H2T + hc * 2 * KB, 1024)

        def HTB(par, strip):
            return bfv(O_HTB + par * 16 * KB + strip * KB, 512)

        def HTB_all(par):
            return vb[:, (O_HTB + par * 16 * KB) // 2:
                      (O_HTB + par * 16 * KB) // 2 + 16 * 512]

        def XTile(par):
            return f32v(O_XT + par * 8 * KB, D)

        XSQ = f32v(O_XSQ, D)
        XSQ2 = f32v(O_XSQ2, D)

        def WQB(sl4):
            return bfv(O_WQB + sl4 * 4 * KB, D)

        WKB = bfv(O_WKB, II)
        WVB = bfv(O_WVB, II)

        def PT(par, kc):
            return bfv(O_PT + par * 32 * KB + kc * 2 * KB, 1024)

        def WOB(dc):
            if dc < 3:
                return bfv(170 * KB + dc * 4 * KB, D)
            if dc < 11:
                return bfv(O_WOB + (dc - 3) * 4 * KB, D)
            return bfv(130 * KB + (dc - 11) * 4 * KB, D)

        def WOB_span(dc0, ndc):
            a = WOB(dc0)
            return vb[:, a.offset: a.offset + ndc * D]

        def DT(sl):
            return bfv(O_DT + sl * 4 * KB, D)

        def HMID(sl):
            return bfv(O_HMID + sl * 2 * KB, 1024)

        def GUB(par, which):
            return bfv(O_GUB + par * 8 * KB + which * 4 * KB, D)

        def RB(i):
            off = (O_RB + i * 2 * KB) // 4
            return vf[0:64, off: off + 512]

        O_CTU = 166 * KB

        def CTU(par):
            return bfv(O_CTU + par * 2 * KB, 1024)

        def XRb(par):
            return f32v((150 if par == 0 else 158) * KB, D)
        BOB = f32v(O_BOB, D)
        DSB = f32v(O_DSB, D)
        GSIL = bfv(O_GSIL, 1024)
        USC = bfv(O_USC, 1024)

        def HB(par):
            return bfv(O_HB + par * 4 * KB, D)

        def HBA(sl4):
            return bfv(162 * KB + sl4 * 4 * KB, D)

        v_op("memset", epsb[:, 0:1], D * EPS)
        v_op("memset", epsb[:, 1:2], 0.5 * float(np.log(D)))
        for t in range(16):
            v_op("memset",
                 V65(t).rearrange("p (k n) -> p k n", k=8)[:, :, 64:65], 1.0)
        v_op("memset", usb[:], 0.0, inc=True)
        xd_ident = y_dma("XDI", identb[:], ident_d[:])

        g_dma("WKV", WKB.rearrange("p (c n) -> p c n", c=16),
              wk.rearrange("(c p) n -> p c n", p=P))
        wd_wkv = g_dma("WKV", WVB.rearrange("p (c n) -> p c n", c=16),
                       wv.rearrange("(c p) n -> p c n", p=P))

        wd_wq = {}

        def prefetch_wq(blk, mi, war_pe):
            g_wait(PE, war_pe)
            wd_wq[(blk, mi)] = ("WQ%d" % (mi % 4), g_dma(
                "WQ%d" % (mi % 4),
                WQB(mi % 4).rearrange("p (c n) -> p c n", c=16),
                wq[:, mi * P:(mi + 1) * P].rearrange("(c p) n -> p c n", p=P)))

        for _m in range(4):
            prefetch_wq(0, _m, 0)

        act_xfree, dve_sfree, dve_norm1, act_rstd = {}, {}, {}, {}
        dve_trevac, dve_var, pe_tr = {}, {}, {}
        pe_q, dve_qevac, pe_k, dve_kevac, pe_v, dve_vevac = {}, {}, {}, {}, {}, {}

        def emit_ln(xtile_ap, st, xsq_ap, hb_ap, war_sq_dve, war_st_act,
                    war_st_dve, war_hb_pe):
            v_wait(ACT, war_st_act)
            dve_s = v_op("reduce_sum", st[:, 0:1], xtile_ap, AX.X, inc=True)
            s_wait(DVE, war_sq_dve)
            C['act'] += 1
            EL["s"].append(lambda e, o=xsq_ap, i=xtile_ap, a=st[:, 1:2]:
                           e.activation(o, i, AF.Square, accum_out=a)
                           .then_inc(ACT, 1))
            a_sq = C['act']
            v_op("tensor_mul", st[:, 4:5], st[:, 0:1], st[:, 0:1],
                 inc=True, chain=True)
            v_wait(ACT, a_sq)
            dve_v = v_op("scalar_tensor_tensor", st[:, 5:6], st[:, 4:5],
                         -1.0 / D, st[:, 1:2], ALU.mult, ALU.add,
                         inc=True, chain=True)
            v_op("tensor_scalar_mul", st[:, 2:3], st[:, 0:1], 1.0 / D,
                 inc=True, chain=True)
            s_wait(DVE, dve_v)
            s_wait(DVE, war_st_dve)
            s_act(st[:, 6:7], st[:, 5:6], AF.Ln, bias=epsb[:, 0:1], inc=True)
            a_r = s_act(st[:, 6:7], st[:, 6:7], AF.Exp, scale=-0.5,
                        bias=epsb[:, 1:2], inc=True, chain=True)
            v_wait(ACT, a_r)
            v_wait(PE, war_hb_pe)
            dve_n = v_op("tensor_scalar", hb_ap, xtile_ap, st[:, 2:3],
                         st[:, 6:7], ALU.subtract, ALU.mult, inc=True,
                         chain=True)
            return dve_n, dve_s, dve_v, a_sq, a_r

        pe_qkv_end = {}

        def ln_compute(g):
            blk, bt = divmod(g, 4)
            par = g % 2
            y_wait(ACT, act_xfree.get(g - 2, 0))
            y_wait(DVE, dve_sfree.get(g - 2, 0))
            xsem = "XD%d" % par
            xd_g = y_dma(xsem, XTile(par), xt[g * P:(g + 1) * P, :])
            st = st0 if par == 0 else st1
            v_wait(xsem, xd_g)
            s_wait(xsem, xd_g)
            d_n, dve_s, dve_v, a_sq, a_r = emit_ln(
                XTile(par), st, XSQ, HBA(g % 4),
                war_sq_dve=dve_var.get(g - 2, 0),
                war_st_act=act_rstd.get(g - 2, 0),
                war_st_dve=dve_norm1.get(g - 2, 0),
                war_hb_pe=pe_tr.get((g - 4, 1), 0))
            dve_norm1[g] = d_n
            act_rstd[g] = a_r
            act_xfree[g] = a_sq
            dve_sfree[g] = d_n
            dve_var[g] = dve_v

        def tr_tile(g):
            blk, bt = divmod(g, 4)
            t_mode("tr")
            if g == 0:
                t_wait("XDI", xd_ident)
            t_wait(DVE, dve_norm1[g])
            for half in range(2):
                t_wait(DVE, dve_trevac.get((g - 1, half), 0))
                for cc in range(8):
                    c8 = half * 8 + cc
                    t_trans(PS_TR[:, half * 1024 + cc * P:
                                  half * 1024 + (cc + 1) * P],
                            HBA(g % 4)[:, c8 * P:(c8 + 1) * P], inc=(cc == 7))
                pe_tr[(g, half)] = C['pe']
                v_wait(PE, pe_tr[(g, half)])
                v_wait(PE, pe_qkv_end.get(blk - 2, 0))
                src_ = PS_TR[:, half * 1024:(half + 1) * 1024] \
                    .rearrange("p (c n) -> p c n", c=8)
                dst = HTB_all(blk % 2).rearrange("p (c n) -> p c n", c=16) \
                    [:, half * 8:(half + 1) * 8, bt * P:(bt + 1) * P]
                dve_trevac[(g, half)] = v_op("tensor_copy", dst, src_,
                                             inc=True)

        def qkv_block(blk, bg=()):
            bg = list(bg)

            def bg_step():
                if bg:
                    ln_compute(bg.pop(0))
            g = blk * 4 + 3
            if blk == 0:
                v_wait("XDC", xd_consts)
            t_mode("mm")
            t_wait(DVE, dve_trevac[(g, 1)])
            if blk == 0:
                t_wait("WKV", wd_wkv)
            hpar = blk % 2
            own = blk < 2

            kvps = [PS_B[:, 0:512], PS_B[:, 512:1024],
                    PS_C[:, 0:512], PS_C[:, 512:1024]]
            for miq in range(4):
                ph = kvps[miq]
                t_wait(DVE, dve_vevac.get((blk - 1, miq), 0))
                for ci in range(16):
                    mm(ph, WKB[:, ci * 512 + miq * P:
                               ci * 512 + miq * P + P],
                       HTB(hpar, ci), start=(ci == 0), stop=(ci == 15),
                       inc=(ci == 15))
                pe_k[(blk, miq)] = C['pe']
                v_wait(PE, pe_k[(blk, miq)])
                dve_kevac[(blk, miq)] = v_op(
                    "tensor_scalar_add",
                    KT(miq)[:, blk * 512:(blk + 1) * 512], ph,
                    bk2b[:, miq:miq + 1], inc=True)
            bg_step()

            for tt_ in range(4):
                gt = blk * 4 + tt_
                ph = kvps[tt_]
                t_wait(DVE, dve_kevac[(blk, tt_)])
                for ci in range(16):
                    mm(ph, HTB(hpar, ci)[:, tt_ * P:(tt_ + 1) * P],
                       WVB[:, ci * 512: ci * 512 + 512],
                       start=(ci == 0), stop=(ci == 15), inc=(ci == 15))
                pe_v[(blk, tt_)] = C['pe']
                v_wait(PE, pe_v[(blk, tt_)])
                dve_vevac[(blk, tt_)] = v_op(
                    "tensor_add",
                    V65(gt).rearrange("p (k n) -> p k n", k=8)[:, :, 0:64],
                    ph.rearrange("p (k n) -> p k n", k=8),
                    bvbb.rearrange("p (k n) -> p k n", k=8), inc=True)
            bg_step()

            if own:
                for miq in range(16):
                    t_wait(*wd_wq[(blk, miq)])
                    ph = PS_A[:, (miq % 2) * 512:(miq % 2) * 512 + 512]
                    if miq < 2:
                        t_wait(DVE, dve_qevac.get((blk - 1, 14 + miq), 0))
                    else:
                        t_wait(DVE, dve_qevac.get((blk, miq - 2), 0))
                    for ci in range(16):
                        mm(ph, WQB(miq % 4)[:, ci * P:(ci + 1) * P],
                           HTB(hpar, ci), start=(ci == 0), stop=(ci == 15),
                           inc=(ci == 15))
                    pe_q[(blk, miq)] = C['pe']
                    nb, nm = (blk, miq + 4) if miq + 4 < 16 \
                        else (blk + 1, miq + 4 - 16)
                    if nb < 2:
                        prefetch_wq(nb, nm, pe_q[(blk, miq)])
                    v_wait(PE, pe_q[(blk, miq)])
                    for hh in range(2):
                        h = 2 * miq + hh
                        pj, base = _pair_of_head(h)
                        dve_qevac[(blk, miq)] = v_op(
                            "tensor_scalar_add",
                            QT(pj)[base:base + 64,
                                   blk * 512:(blk + 1) * 512],
                            ph[hh * 64:(hh + 1) * 64, :],
                            bq2b[hh * 64:(hh + 1) * 64, miq:miq + 1],
                            inc=(hh == 1))
                    if miq in (3, 7):
                        bg_step()
            while bg:
                bg_step()
            pe_qkv_end[blk] = C['pe']

        for g in range(4):
            ln_compute(g)
        y_dma("XDC", onesb[:], ones_d[:])
        y_dma("XDC", bq2b[:], bq2[:])
        y_dma("XDC", bk2b[:], bk2[:])
        y_dma("XDC", bvbb[:], bvb[:])
        y_dma("XDC", gs2b[:], gs2[:])
        xd_consts = y_dma("XDC", us2b[:], us2[:])
        for g in range(4):
            tr_tile(g)
            ln_compute(g + 4)
        for g in range(4, 8):
            tr_tile(g)
        qkv_block(0, bg=(8, 9, 10, 11))
        for g in range(8, 12):
            tr_tile(g)
        qkv_block(1, bg=(12, 13, 14, 15))
        for g in range(12, 16):
            tr_tile(g)
        qkv_block(2)
        qkv_block(3)

        pe_end_A, act_end_A, dve_end_A = C['pe'], C['act'], C['dve']

        if debug is not None and debug["phase"] == "A":
            g_wait(ACT, act_end_A)
            g_wait(DVE, dve_end_A)
            g_wait(PE, pe_end_A)
            for pj in range(16):
                dbg_dma(dbg_outs["qt"][pj * P:(pj + 1) * P, :], QT(pj))
            for j2 in range(4):
                dbg_dma(dbg_outs["kt"][j2 * P:(j2 + 1) * P, :], KT(j2))
            for t in range(16):
                dbg_dma(dbg_outs["v65"][t * P:(t + 1) * P, :], V65(t))
            EL["g"].append(lambda e, v=C['od']: e.wait_ge(OD, v))
            return _finish(nc, EL)

        t_wait(ACT, act_end_A)
        t_wait(DVE, dve_end_A)
        s_wait(PE, pe_end_A)
        s_wait(DVE, dve_end_A)

        g_wait(PE, pe_end_A)
        g_wait(DVE, dve_end_A)
        wd_wo_early = g_dma(
            "WWO", WOB_span(0, 3).rearrange("p (c n) -> p c n", c=3),
            wo[0:384, :].rearrange("(c p) n -> p c n", p=P))

        act_exp, pe_ctx, pe_sc, pe_ctx4 = {}, {}, {}, {}
        dve_rinv, xd_rb, dve_norm = {}, {}, {}
        dve_ctev, xd_rout = {}, {}
        units = [(qh, pj) for qh in range(2) for pj in range(16)]
        actB0 = C['act']

        def _aexp(u, kc):
            if u < 0:
                return 0
            return actB0 + 16 * u + kc + 1

        def emit_unit(u):
            qh, pj = units[u]
            j2 = pj // 4
            if u >= 1:
                qh1, pj1 = units[u - 1]
                hA1, hB1 = _heads_of_pair(pj1)
                kvA1, kvB1 = 2 * (pj1 // 4), 2 * (pj1 // 4) + 1
            for kc in range(16):
                gidx = 16 * u + kc
                pp = PS_ROT[gidx % 3]
                t_wait(ACT, _aexp((gidx - 3) // 16, (gidx - 3) % 16))
                mm(pp[:, 0:512], KT(j2)[0:64, kc * P:(kc + 1) * P],
                   QT(pj)[0:64, qh * 512:(qh + 1) * 512],
                   start=True, stop=True)
                mm(pp[:, 512:1024], KT(j2)[64:128, kc * P:(kc + 1) * P],
                   QT(pj)[64:128, qh * 512:(qh + 1) * 512],
                   start=True, stop=True, inc=True)
                pe_sc[(u, kc)] = C['pe']
                if u >= 1:
                    t_wait(ACT, act_exp[(u - 1, kc)])
                    if kc == 0:
                        t_wait(DVE, dve_ctev.get(u - 2, 0))
                    mm(PS_C[0:65, 0:512],
                       V65(kc)[:, kvA1 * 65: kvA1 * 65 + 65],
                       PT((u - 1) % 2, kc)[:, 0:512],
                       start=(kc == 0), stop=(kc == 15))
                    mm(PS_C[0:65, 512:1024],
                       V65(kc)[:, kvB1 * 65: kvB1 * 65 + 65],
                       PT((u - 1) % 2, kc)[:, 512:1024],
                       start=(kc == 0), stop=(kc == 15),
                       inc=(kc % 4 == 3))
                    if kc % 4 == 3:
                        pe_ctx4[(u - 1, kc // 4)] = C['pe']
            if u >= 1:
                pe_ctx[u - 1] = C['pe']

        def emit_exp(u):
            for kc in range(16):
                if u >= 2:
                    s_wait(PE, pe_ctx4.get((u - 2, kc // 4), 0))
                s_wait(PE, pe_sc[(u, kc)])
                pp = PS_ROT[(16 * u + kc) % 3]
                act_exp[(u, kc)] = s_act(PT(u % 2, kc), pp[:], AF.Exp,
                                         scale=0.125, inc=True)
                assert act_exp[(u, kc)] == _aexp(u, kc)

        def emit_norm(u):
            qh, pj = units[u]
            hA, hB = _heads_of_pair(pj)
            v_wait(PE, pe_ctx[u])
            v_wait(DVE, dve_norm.get(u - 2, 0))
            v_op("tensor_copy", CTU(u % 2)[0:64, 0:512], PS_C[0:64, 0:512],
                 inc=True)
            v_op("tensor_copy", CTU(u % 2)[0:64, 512:1024],
                 PS_C[0:64, 512:1024], inc=True)
            v_wait("XDR", xd_rout.get(u - 1, 0))
            v_op("tensor_copy", rrow[0:1, 0:512], PS_C[64:65, 0:512], inc=True)
            dve_ctev[u] = v_op("tensor_copy", rrow[0:1, 512:1024],
                               PS_C[64:65, 512:1024], inc=True)
            slot = u % 2
            y_wait(DVE, dve_ctev[u])
            xd_rout[u] = y_dma("XDR", scr_r[slot:slot + 1, :], rrow[0:1, :])
            y_wait("XDR", xd_rout[u])
            y_wait(DVE, dve_rinv.get(u - 1, 0))
            xd1 = y_dma("XDR", rseg[:, :],
                        bass.AP(scr_r, slot * 1024, [[8, 128], [1, 8]]))
            v_wait("XDR", xd1)
            dve_rinv[u] = v_op("reciprocal", rsegi[:, :], rseg[:, :], inc=True)
            y_wait(DVE, dve_rinv[u])
            xd2 = y_dma("XDR", bass.AP(scr_r, (2 + slot) * 1024,
                                       [[8, 128], [1, 8]]), rsegi[:, :])
            y_wait("XDR", xd2)
            y_wait(DVE, dve_norm.get(u - 1, 0))
            y_dma("XDR", RB(0), bass.AP(scr_r, (2 + slot) * 1024,
                                        [[0, 64], [1, 512]]))
            xd_rb[u] = y_dma("XDR", RB(1), bass.AP(scr_r,
                                                   (2 + slot) * 1024 + 512,
                                                   [[0, 64], [1, 512]]))
            v_wait("XDR", xd_rb[u])
            for h, rb, col0 in ((hA, RB(0), 0), (hB, RB(1), 512)):
                dc, base = h // 2, (h % 2) * 64
                dve_norm[u] = v_op(
                    "tensor_mul",
                    CTXT(dc)[base:base + 64, qh * 512:(qh + 1) * 512],
                    CTU(u % 2)[0:64, col0:col0 + 512], rb,
                    inc=(col0 == 512), chain=(col0 == 0))

        t_mode("mm")
        emit_unit(0)
        emit_exp(0)
        for u in range(1, 32):
            emit_unit(u)
            emit_exp(u)
            emit_norm(u - 1)
        qh31, pj31 = units[31]
        hA31, hB31 = _heads_of_pair(pj31)
        kvA31, kvB31 = 2 * (pj31 // 4), 2 * (pj31 // 4) + 1
        for kc in range(16):
            t_wait(ACT, act_exp[(31, kc)])
            if kc == 0:
                t_wait(DVE, dve_ctev.get(30, 0))
            mm(PS_C[0:65, 0:512], V65(kc)[:, kvA31 * 65: kvA31 * 65 + 65],
               PT(31 % 2, kc)[:, 0:512], start=(kc == 0), stop=(kc == 15))
            mm(PS_C[0:65, 512:1024], V65(kc)[:, kvB31 * 65: kvB31 * 65 + 65],
               PT(31 % 2, kc)[:, 512:1024],
               start=(kc == 0), stop=(kc == 15), inc=(kc % 4 == 3))
        pe_ctx[31] = C['pe']
        emit_norm(31)

        g_wait(PE, pe_ctx[30])
        wd_wo_mid = g_dma(
            "WWO2", WOB_span(3, 8).rearrange("p (c n) -> p c n", c=8),
            wo[384:1408, :].rearrange("(c p) n -> p c n", p=P))
        g_wait(PE, pe_ctx[31])
        wd_wo_late = g_dma(
            "WWO3", WOB_span(11, 5).rearrange("p (c n) -> p c n", c=5),
            wo[1408:2048, :].rearrange("(c p) n -> p c n", p=P))

        if debug is not None and debug["phase"] == "B":
            g_wait(DVE, dve_norm[31])
            for dc in range(16):
                dbg_dma(dbg_outs["ctxt"][dc * P:(dc + 1) * P, :], CTXT(dc))
            EL["g"].append(lambda e, v=C['od']: e.wait_ge(OD, v))
            return _finish(nc, EL)

        pass
        y_wait(PE, pe_end_A)
        xd_bob = y_dma("XDC", BOB, bob[:])

        norm_bydc = {}
        for uu in range(32):
            _, pju = units[uu]
            for hh_ in _heads_of_pair(pju):
                dcu = hh_ // 2
                norm_bydc[dcu] = max(norm_bydc.get(dcu, 0), dve_norm[uu])

        xd_xr, pe_wo, dve_x2 = {}, {}, {}
        t_mode("mm")
        t_wait(ACT, act_exp[(31, 15)])
        for tm in range(8):
            if tm == 0:
                y_wait(PE, pe_ctx[31])
            if tm == 1:
                y_wait(DVE, dve_norm[31])
            y_wait(DVE, dve_x2.get(tm - 2, 0))
            xd_xr[tm] = y_dma("XD%d" % (tm % 2), XRb(tm % 2),
                              xt[tm * P:(tm + 1) * P, :])
            t_wait(DVE, dve_x2.get(tm - 2, 0))
            if tm == 0:
                t_wait("WWO", wd_wo_early)
            pset = (PS_A, PS_B) if tm % 2 == 0 else (PS_D, PS_C)
            if tm == 1:
                t_wait(DVE, dve_ctev[31])
            for dc in range(16):
                if tm == 0:
                    t_wait(DVE, norm_bydc[dc])
                    if dc == 3:
                        t_wait("WWO2", wd_wo_mid)
                    if dc == 11:
                        t_wait("WWO3", wd_wo_late)
                for n in range(4):
                    ph = pset[0] if n < 2 else pset[1]
                    mm(ph[:, (n % 2) * 512:(n % 2) * 512 + 512],
                       CTXT(dc)[:, tm * P:(tm + 1) * P],
                       WOB(dc)[:, n * 512:(n + 1) * 512],
                       start=(dc == 0), stop=(dc == 15),
                       inc=(dc == 15 and n == 3))
            pe_wo[tm] = C['pe']
            v_wait(PE, pe_wo[tm])
            v_wait("XD%d" % (tm % 2), xd_xr[tm])
            if tm == 0:
                v_wait("XDC", xd_bob)
            for half in range(2):
                ph = pset[half]
                sl = slice(half * 1024, half * 1024 + 1024)
                v_op("scalar_tensor_tensor", X2(tm)[:, sl], ph[:], 1.0,
                     BOB[:, sl], ALU.mult, ALU.add, inc=True)
                dve_x2[tm] = v_op("tensor_add", X2(tm)[:, sl], X2(tm)[:, sl],
                                  XRb(tm % 2)[:, sl], inc=True, chain=True)

        if debug is not None and debug["phase"] == "C":
            g_wait(DVE, dve_x2[7])
            for tm in range(8):
                dbg_dma(dbg_outs["x2"][tm * P:(tm + 1) * P, :], X2(tm))
            EL["g"].append(lambda e, v=C['od']: e.wait_ge(OD, v))
            return _finish(nc, EL)

        y_wait(DVE, dve_x2[7])
        y_wait(PE, pe_wo[7])
        xd_dsb = y_dma("XDC", DSB, dsb_d[:])

        dve_norm2, act_rstd2, dve_tr2, dve_var2, pe_tr2 = {}, {}, {}, {}, {}
        ST4 = [st0, st1, st2, st3]

        def HBD(sl):
            if sl < 2:
                return bfv(174 * KB + sl * 4 * KB, D)
            return bfv(130 * KB + (sl - 2) * 4 * KB, D)

        def ln2_compute(tm):
            st = ST4[tm % 4]
            v_wait(DVE, dve_x2[7])
            s_wait(DVE, dve_x2[7])
            war_act = act_rstd2.get(tm - 4, 0)
            war_dve = dve_norm2.get(tm - 4, 0)
            if tm < 2:
                war_act = max(war_act, act_rstd[14 + tm])
                war_dve = max(war_dve, dve_norm1[14 + tm])
            d_n, _, dve_v, _, a_r = emit_ln(
                X2(tm), st, XSQ2, HBD(tm % 4),
                war_sq_dve=dve_var2.get(tm - 4, 0),
                war_st_act=war_act,
                war_st_dve=war_dve,
                war_hb_pe=pe_tr2.get(tm - 4, 0))
            dve_norm2[tm] = d_n
            act_rstd2[tm] = a_r
            dve_var2[tm] = dve_v

        def tr2_tile(tm):
            t_mode("tr")
            t_wait(DVE, dve_norm2[tm])
            for half in range(2):
                t_wait(DVE, dve_tr2.get((tm - 1, half), 0))
                for cc in range(8):
                    c8 = half * 8 + cc
                    t_trans(PS_TR[:, half * 1024 + cc * P:
                                  half * 1024 + (cc + 1) * P],
                            HBD(tm % 4)[:, c8 * P:(c8 + 1) * P],
                            inc=(cc == 7))
                pe_tr2[tm] = C['pe']
                v_wait(PE, C['pe'])
                src = PS_TR[:, half * 1024:(half + 1) * 1024] \
                    .rearrange("p (c n) -> p c n", c=8)
                dst = vb[:, O_H2T // 2: O_H2T // 2 + 16 * 1024] \
                    .rearrange("p (c n) -> p c n", c=16) \
                    [:, half * 8:(half + 1) * 8, tm * P:(tm + 1) * P]
                dve_tr2[(tm, half)] = v_op("tensor_copy", dst, src, inc=True)

        for tm in range(3):
            ln2_compute(tm)
        for tm in range(3, 8):
            tr2_tile(tm - 3)
            ln2_compute(tm)
        for tm in range(5, 8):
            tr2_tile(tm)

        if debug is not None and debug["phase"] == "D1":
            g_wait(DVE, dve_tr2[(7, 1)])
            for hc in range(16):
                dbg_dma(dbg_outs["h2t"][hc * P:(hc + 1) * P, :], H2T(hc))
            EL["g"].append(lambda e, v=C['od']: e.wait_ge(OD, v))
            return _finish(nc, EL)

        if debug is not None and debug["phase"] == "D3b":
            v_wait(DVE, dve_tr2[(7, 1)])
            srcs = [PS_A[:, 0:512], PS_A[:, 512:1024], PS_B[:, 0:512],
                    PS_B[:, 512:1024], PS_C[:, 0:512], PS_C[:, 512:1024]]
            for i6, s6 in enumerate(srcs):
                if i6 > 0:
                    EL["v"].append(lambda e, v=C['od']: e.wait_ge(OD, v))
                dve_a = v_op("tensor_copy", dscr[:, 0:512], s6, inc=True)
                g_wait(DVE, dve_a)
                dbg_dma(dbg_outs["ps6"][:, i6 * 512:(i6 + 1) * 512],
                        dscr[:, 0:512])
            EL["g"].append(lambda e, v=C['od']: e.wait_ge(OD, v))
            return _finish(nc, EL)

        t_mode("mm")
        t_wait(DVE, dve_tr2[(7, 1)])
        t_wait(DVE, dve_x2[7])
        v_wait("XDC", xd_dsb)

        pe_g, pe_u, act_silu, act_ucopy, dve_hmid = {}, {}, {}, {}, {}
        pe_dwn, dve_devac, dve_uacc = {}, {}, {}
        dve_ufin, dve_fin = {}, {}
        wd_gu, wd_dt = {}, {}

        def prefetch_gu(ic):
            par = ic % 2
            gsem = "WG%d" % par
            if ic < 2:
                g_wait(PE, pe_wo[7])
                g_wait(DVE, dve_x2[7])
            g_wait(PE, pe_g.get(ic - 2, 0))
            g_dma(gsem, GUB(par, 0).rearrange("p (c n) -> p c n", c=16),
                  gwt[:, ic * P:(ic + 1) * P].rearrange("(c p) n -> p c n",
                                                        p=P))
            g_wait(PE, pe_u.get(ic - 2, 0))
            wd_gu[ic] = (gsem, g_dma(
                gsem, GUB(par, 1).rearrange("p (c n) -> p c n", c=16),
                uwt[:, ic * P:(ic + 1) * P].rearrange("(c p) n -> p c n",
                                                      p=P)))

        def prefetch_dt(grp, sl):
            ic = grp * 8 + sl
            if grp == 0:
                g_wait(PE, pe_wo[7])
            else:
                g_wait(PE, pe_dwn.get((grp - 1, 7, 1), 0))
            wd_dt[ic] = g_dma("WDT", DT(sl), dwt[ic * P:(ic + 1) * P, :])

        prefetch_gu(0)
        prefetch_gu(1)

        for grp in range(8):
            for sl in range(8):
                prefetch_dt(grp, sl)
            for sl in range(8):
                ic = grp * 8 + sl
                par = ic % 2
                t_wait(*wd_gu[ic])
                t_wait(ACT, act_silu.get(ic - 1, 0))
                if sl == 0 and grp > 0:
                    t_wait(DVE, dve_devac[(grp - 1, 7, 1)])
                for hc in range(16):
                    for nn in range(2):
                        mm(PS_A[:, nn * 512: nn * 512 + 512],
                           GUB(par, 0)[:, hc * P:(hc + 1) * P],
                           H2T(hc)[:, nn * 512: nn * 512 + 512],
                           start=(hc == 0), stop=(hc == 15),
                           inc=(hc == 15 and nn == 1))
                pe_g[ic] = C['pe']
                if debug is not None and debug["phase"] == "D3" and ic == 0:
                    v_wait(PE, pe_g[0])
                    dve_a = v_op("tensor_copy", dscr[:], PS_A[:], inc=True)
                    g_wait(DVE, dve_a)
                    dbg_dma(dbg_outs["psg"][:, :], dscr[:])
                    EL["g"].append(lambda e, v=C['od']: e.wait_ge(OD, v))
                    return _finish(nc, EL)
                t_wait(ACT, act_ucopy.get(ic - 1, 0))
                if sl == 0:
                    t_wait(DVE, dve_uacc.get(grp - 1, 0))
                for hc in range(16):
                    for nn in range(2):
                        mm(PS_B[:, nn * 512: nn * 512 + 512],
                           GUB(par, 1)[:, hc * P:(hc + 1) * P],
                           H2T(hc)[:, nn * 512: nn * 512 + 512],
                           start=(hc == 0), stop=(hc == 15),
                           inc=(hc == 15 and nn == 1))
                pe_u[ic] = C['pe']
                if ic + 2 < 64:
                    prefetch_gu(ic + 2)
                s_wait(PE, pe_g[ic])
                s_wait(DVE, dve_hmid.get(ic - 1, 0))
                act_silu[ic] = s_act(GSIL, PS_A[:], AF.Silu,
                                     scale=gs2b[:, ic:ic + 1], inc=True)
                s_wait(PE, pe_u[ic])
                act_ucopy[ic] = s_act(USC, PS_B[:], AF.Copy,
                                      scale=us2b[:, ic:ic + 1], inc=True)
                v_wait(ACT, act_ucopy[ic])
                if sl == 0 and grp > 0:
                    v_wait(PE, pe_dwn[(grp - 1, 7, 1)])
                dve_hmid[ic] = v_op("tensor_mul", HMID(sl), GSIL, USC,
                                    inc=True)
                if debug is not None and debug["phase"] == "D2" and ic == 0:
                    v_wait(PE, pe_g[0])
                    dve_a = v_op("tensor_copy", dscr[:], PS_A[:], inc=True)
                    g_wait(DVE, dve_a)
                    dbg_dma(dbg_outs["psg"][:, :], dscr[:])
                    v_wait(PE, pe_u[0])
                    g_wait(DVE, dve_a)
                    dve_b = v_op("tensor_copy", dscr[:], PS_B[:], inc=True)
                    g_wait(DVE, dve_b)
                    dbg_dma(dbg_outs["psu"][:, :], dscr[:])
                    g_wait(DVE, dve_hmid[0])
                    dbg_dma(dbg_outs["gsil"][:, :], GSIL)
                    dbg_dma(dbg_outs["usc"][:, :], USC)
                    dbg_dma(dbg_outs["hmid0"][:, :], HMID(0))
                    dbg_dma(dbg_outs["gub0"][:, :], GUB(0, 0))
                    dbg_dma(dbg_outs["gub1"][:, :], GUB(0, 1))
                    EL["g"].append(lambda e, v=C['od']: e.wait_ge(OD, v))
                    return _finish(nc, EL)

            t_wait(DVE, dve_hmid[grp * 8 + 7])
            t_wait(ACT, act_ucopy[grp * 8 + 7])
            t_wait("WDT", wd_dt[grp * 8 + 7])
            for tm in range(8):
                for dh in range(2):
                    pd = PS_C if dh == 0 else PS_D
                    t_wait(DVE, dve_devac.get(
                        (grp, tm - 1, dh),
                        dve_devac.get((grp - 1, 7, dh), 0)))
                    for sl in range(8):
                        st_ap = HMID(sl)[:, tm * P:(tm + 1) * P]
                        for nn in range(2):
                            mm(pd[:, nn * 512: nn * 512 + 512], st_ap,
                               DT(sl)[:, dh * 1024 + nn * 512:
                                      dh * 1024 + (nn + 1) * 512],
                               start=(sl == 0), stop=(sl == 7),
                               inc=(dh == 1 and sl == 7 and nn == 1))
                        if dh == 0:
                            if grp == 7 and sl == 0:
                                t_wait(DVE, dve_ufin.get(tm - 1, 0))
                            mm(PS_B[:, tm:tm + 1], st_ap, onesb[0:P, 0:1],
                               start=(sl == 0), stop=(sl == 7),
                               inc=(sl == 7))
                    pe_dwn[(grp, tm, dh)] = C['pe']
                    if grp == 7 and dh == 0:
                        v_wait(PE, pe_dwn[(7, tm, 0)])
                        v_op("tensor_add", usb[:, tm:tm + 1],
                             usb[:, tm:tm + 1], PS_B[:, tm:tm + 1],
                             inc=True, chain=True)
                        dve_ufin[tm] = v_op(
                            "tensor_scalar_mul", usb[:, tm:tm + 1],
                            usb[:, tm:tm + 1], -128.0, inc=True, chain=True)
                    v_wait(PE, pe_dwn[(grp, tm, dh)])
                    sl_ = slice(dh * 1024, dh * 1024 + 1024)
                    v_op("tensor_mul", dscr[:], pd[:], DSB[:, sl_],
                         inc=True, chain=True)
                    dve_devac[(grp, tm, dh)] = v_op(
                        "tensor_add", X2(tm)[:, sl_], X2(tm)[:, sl_],
                        dscr[:], inc=True, chain=True)
                    if grp == 7 and dh == 1:
                        dve_fin[tm] = v_op(
                            "scalar_tensor_tensor", X2(tm)[:], DSB[:],
                            usb[:, tm:tm + 1], X2(tm)[:],
                            ALU.mult, ALU.add, inc=True, chain=True)
                        y_wait(DVE, dve_fin[tm])
                        o_dma(out_d[tm * P:(tm + 1) * P, :], X2(tm))
            if grp < 7:
                v_wait(PE, pe_dwn[(grp, 7, 0)])
                dve_uacc[grp] = v_op("tensor_add", usb[:], usb[:],
                                     PS_B[:, 0:8], inc=True, chain=True)

        EL["y"].append(lambda e, v=C['od']: e.wait_ge(OD, v))

        return _finish(nc, EL)


def _finish(nc, EL):
    with nc.Block() as block:
        @block.tensor
        def _(eng):
            for f in EL["t"]:
                f(eng)

        @block.vector
        def _(eng):
            for f in EL["v"]:
                f(eng)

        @block.scalar
        def _(eng):
            for f in EL["s"]:
                f(eng)

        @block.sync
        def _(eng):
            for f in EL["y"]:
                f(eng)

        @block.gpsimd
        def _(eng):
            for f in EL["g"]:
                f(eng)
    return nc


_cache = {}


def _get_program(debug=None):
    key = None if debug is None else debug["phase"]
    if key not in _cache:
        _cache[key] = build_program(debug)
    return _cache[key]


def make_in_maps(inputs):
    x = np.asarray(inputs["L0_attn_x"], np.float32)
    asc = np.ascontiguousarray
    bq = np.asarray(inputs["L0_attn_bq"], np.float32)
    bk = np.asarray(inputs["L0_attn_bk"], np.float32)
    bv = np.asarray(inputs["L0_attn_bv"], np.float32)
    bo = np.asarray(inputs["L0_attn_bo"], np.float32)
    gs = np.asarray(inputs["L0_ffn_gate_scale"], np.float32)
    us = np.asarray(inputs["L0_ffn_up_scale"], np.float32)
    ds = np.asarray(inputs["L0_ffn_down_scale"], np.float32)

    shared = dict(
        wq=asc(np.asarray(inputs["L0_attn_Wq"], np.float32)),
        wk=asc(np.asarray(inputs["L0_attn_Wk"], np.float32)),
        wv=asc(np.asarray(inputs["L0_attn_Wv"], np.float32)),
        wo=asc(np.asarray(inputs["L0_attn_Wo"], np.float32)),
        bq2=asc(bq.reshape(16, P).T),
        bk2=asc(bk.reshape(4, P).T),
        bvb=asc(np.broadcast_to(bv, (P, KVD))),
        bob=asc(np.broadcast_to(bo, (P, D))),
        gwt=asc(np.asarray(inputs["L0_ffn_gate_w"], np.int32).T),
        uwt=asc(np.asarray(inputs["L0_ffn_up_w"], np.int32).T),
        dwt=asc(np.asarray(inputs["L0_ffn_down_w"], np.int32).T),
        gs2=asc(gs.reshape(64, P).T),
        us2=asc(us.reshape(64, P).T),
        dsb=asc(np.broadcast_to(ds, (P, D))),
        ident=np.eye(P, dtype=ml_dtypes.bfloat16),
        ones=np.ones((P, 512), dtype=ml_dtypes.bfloat16),
    )
    in_maps = []
    for c in range(8):
        b, half = divmod(c, 2)
        xc = asc(np.concatenate(
            [x[b, half * TOK:(half + 1) * TOK],
             x[b, (1 - half) * TOK:(2 - half) * TOK]], axis=0))
        m = dict(shared)
        m["xt"] = xc
        in_maps.append(m)
    return in_maps


def kernel(**inputs):
    nc = _get_program()
    in_maps = make_in_maps(inputs)
    res = run_bass_kernel_spmd(nc, in_maps, core_ids=list(range(8)))
    out = np.empty((4, S, D), np.float32)
    for c in range(8):
        b, half = divmod(c, 2)
        out[b, half * TOK:(half + 1) * TOK] = res.results[c]["out"]
    return out
